# version 24
# baseline (speedup 1.0000x reference)
import contextlib
import numpy as np
import ml_dtypes

import concourse.bass as bass
import concourse.mybir as mybir
from concourse.bass_utils import run_bass_kernel_spmd

F32 = mybir.dt.float32
BF16 = mybir.dt.bfloat16
I8 = mybir.dt.int8
AF = mybir.ActivationFunctionType
ALU = mybir.AluOpType
AX = mybir.AxisListType
BF = ml_dtypes.bfloat16

DEBUG = False
NO_ARENA = False
TRUNC = 0
S, T, D, H, HD = 4, 2048, 1024, 16, 64
W = T // 128
CH = D // 128
SCALE = HD ** -0.5
EPS = 1e-5
MAGIC = float(3 * 2 ** 22)


def build_nc():
    nc = bass.Bass("TRN2", target_bir_lowering=False, detect_race_conditions=False)

    xin = nc.dram_tensor("xin", [S * T, D], F32, kind="ExternalInput").ap()
    wnames = ["wq1t", "wk1t", "wq2t", "wk2t", "wvt", "wgt", "wpt"]
    wdram = {n: nc.dram_tensor(n, [D, D], F32, kind="ExternalInput").ap() for n in wnames}
    fwt = nc.dram_tensor("fwt", [D, D], I8, kind="ExternalInput").ap()
    bias_d = {n: nc.dram_tensor(n, [128, CH], F32, kind="ExternalInput").ap()
              for n in ["qb1c", "kb1c", "vbc", "gbc"]}
    row_d = {n: nc.dram_tensor(n, [1, D], F32, kind="ExternalInput").ap()
             for n in ["b2q", "b2k", "bp"]}
    bias_rep_d = nc.dram_tensor("bias_rep", [128, D], F32, kind="ExternalInput").ap()
    ws_rep_d = nc.dram_tensor("ws_rep", [128, D], F32, kind="ExternalInput").ap()
    ident_bf_d = nc.dram_tensor("ident_bf", [128, 128], BF16, kind="ExternalInput").ap()
    ident_f_d = nc.dram_tensor("ident_f", [128, 128], F32, kind="ExternalInput").ap()
    mask_f_d = nc.dram_tensor("mask_f", [128, 128], F32, kind="ExternalInput").ap()
    ipat_f_d = nc.dram_tensor("ipat_f", [128, 128], F32, kind="ExternalInput").ap()
    ones2_d = nc.dram_tensor("ones2", [128, 2], F32, kind="ExternalInput").ap()
    onesr_bf_d = nc.dram_tensor("onesr_bf", [1, 128], BF16, kind="ExternalInput").ap()
    onesr_f_d = nc.dram_tensor("onesr_f", [1, 128], F32, kind="ExternalInput").ap()
    kcol_d = nc.dram_tensor("kcol", [128, 4], F32, kind="ExternalInput").ap()

    out_d = nc.dram_tensor("out", [S * T, D], F32, kind="ExternalOutput").ap()
    scr_xn2 = nc.dram_tensor("scr_xn2", [S * T, D], F32).ap()
    scr_xb = nc.dram_tensor("scr_xb", [S * T, D], F32).ap()
    cc_in = nc.dram_tensor("cc_in", [1, 1], F32)
    cc_out = nc.dram_tensor("cc_out", [8, 1], F32)
    taps = {}
    if DEBUG:
        for tn, shp, dt in [("tap_xn", [128, CH * T], BF16), ("tap_q1", [128, CH * T], BF16),
                            ("tap_k1", [128, CH * T], BF16), ("tap_Asb", [128, D], F32),
                            ("tap_A2", [128, D], BF16), ("tap_zz", [128, 8], F32),
                            ("tap_v", [128, CH * T], BF16), ("tap_g", [128, CH * T], BF16),
                            ("tap_E", [128, CH * T], BF16), ("tap_Es", [128, CH * T], BF16),
                            ("tap_stats", [128, 80], F32), ("tap_xp", [128, D], F32),
                            ("tap_qtm", [128, D], BF16), ("tap_ktm", [128, D], BF16),
                            ("tap_sxn2", [S * T, D], F32), ("tap_sxb", [S * T, D], F32)]:
            taps[tn] = nc.dram_tensor(tn, shp, dt, kind="ExternalOutput").ap()

    est = contextlib.ExitStack()
    sbt = lambda name, shape, dt: est.enter_context(nc.sbuf_tensor(name, shape, dt))
    pst = lambda name, shape: est.enter_context(nc.psum_tensor(name, shape, F32))

    bufA = sbt("bufA", [128, CH * T], BF16)
    bufB = sbt("bufB", [128, CH * T], BF16)
    bufC = sbt("bufC", [128, CH * T], BF16)
    bufD = sbt("bufD", [128, CH * T], BF16)
    arena = [sbt(f"wslot{i}", [128, CH * D], BF16) for i in range(2)]

    xt = [sbt(f"xt{i}", [128, D], F32) for i in range(2)]
    qtm = [sbt(f"qtm{i}", [128, D], BF16) for i in range(2)]
    ktm = [sbt(f"ktm{i}", [128, D], BF16) for i in range(2)]
    xnt = [sbt(f"xnt{i}", [128, D], BF16) for i in range(2)]
    scrf = sbt("scrf", [128, D], F32)
    A_sb = sbt("A_sb", [128, D], F32)
    A2bf = sbt("A2bf", [128, D], BF16)
    stats = sbt("stats", [128, 80], F32)
    st2 = sbt("st2", [128, 8], F32)
    zz = sbt("zz", [128, 8], F32)
    rmax = sbt("rmax", [128, 1], F32)
    gtiny = sbt("gtiny", [1, 16], F32)
    xs_col = sbt("xs_col", [128, 1], F32)
    sepb = sbt("sepb", [128, 32], F32)
    xinv_col = sbt("xinv_col", [128, 1], F32)

    ident_bf = sbt("identbf", [128, 128], BF16)
    ident_f = sbt("identf", [128, 128], F32)
    mask_f = sbt("maskf", [128, 128], F32)
    ipat_f = sbt("ipatf", [128, 128], F32)
    ones2 = sbt("sones2", [128, 2], F32)
    onesr_bf = sbt("sonesrbf", [1, 128], BF16)
    onesr_f = sbt("sonesrf", [1, 128], F32)
    kcol = sbt("skcol", [128, 4], F32)
    bcols = {n: sbt("s" + n, [128, CH], F32) for n in ["qb1c", "kb1c", "vbc", "gbc"]}
    brows = {n: sbt("s" + n, [1, D], BF16) for n in ["b2q", "b2k", "bp"]}
    bias_rep = sbt("biasrep", [128, D], F32)
    ws_rep = sbt("wsrep", [128, D], F32)
    wsx = bias_rep.ap()

    assert nc.sbuf_bytes_remaining >= 0, nc.sbuf_bytes_remaining

    bufA_f = bufA.ap().bitcast(F32)
    xp_t = [bufA_f[:, i * D:(i + 1) * D] for i in range(2)]
    xn2_t = [bufA_f[:, (2 + i) * D:(3 + i) * D] for i in range(2)]
    xb_t = [bufA_f[:, (4 + i) * D:(5 + i) * D] for i in range(2)]
    bufD_f = bufD.ap().bitcast(F32)
    lnx = [bufD_f[:, (4 + i) * D:(5 + i) * D] for i in range(4)]
    c_ln2 = [bufD_f[:, 2 * i * D:(2 * i + 2) * D] for i in range(2)]
    bufD_b = bufD.ap()
    c_xq = [bufD_b[:, (8 + 2 * i) * D:(10 + 2 * i) * D] for i in range(2)]
    c_xqf = [bufD_b[:, (12 + 2 * i) * D:(14 + 2 * i) * D] for i in range(2)]
    fwi8 = bufC.ap().bitcast(I8)[:, 0:CH * D]
    bufC_f = bufC.ap().bitcast(F32)
    cscr2 = bufC_f[:, 2 * D:4 * D]
    c_xb = [bufC_f[:, (4 + 2 * i) * D:(6 + 2 * i) * D] for i in range(2)]
    bufA_f2 = bufA.ap().bitcast(F32)
    c_out = [bufA_f2[:, 2 * i * D:(2 * i + 2) * D] for i in range(2)]

    pBig = pst("pBig", [128, 1024])
    pBig2 = pst("pBig2", [128, 1024])
    pP = pst("pP", [128, 512])
    pT = pst("pT", [128, 512])
    pA = pst("pA", [128, 1024])
    assert nc.psum_banks_remaining == 0

    dma_sems = ["KC", "KB0", "KB1", "KB2", "KF", "KG", "A0", "A1", "X0", "X1",
                "XL", "SC", "N0", "N1", "M0", "M1", "L0", "L1", "B0", "B1", "O0", "O1"]
    eng_sems = ["TE", "VE", "SE", "CC", "GE"]
    sems = {n: est.enter_context(nc.semaphore(n)) for n in dma_sems + eng_sems}
    cnt = {n: 0 for n in sems}
    rel = {}

    def inc(instr, s, n=1):
        instr.then_inc(sems[s], n)
        cnt[s] += n

    def wt(eng, *ss):
        for s in ss:
            eng.wait_ge(sems[s], cnt[s])

    def use(eng, *res):
        for r in res:
            for s, c in rel.get(r, ()):
                eng.wait_ge(sems[s], c)

    def release(r, s):
        rel[r] = [(s, cnt[s])]

    def release_multi(r, ss):
        rel[r] = [(s, cnt[s]) for s in ss]

    def use_l(eng, names):
        for n in names:
            use(eng, n)

    def rel_l(names, s):
        for n in names:
            release(n, s)

    blk = est.enter_context(nc.Block())

    class _RecInstr:
        __slots__ = ("incs",)
        def __init__(self):
            self.incs = []
        def then_inc(self, sem, n=1):
            self.incs.append((sem, n))
            return self

    class _RecEngine:
        __slots__ = ("_calls",)
        def __init__(self, calls):
            self._calls = calls
        def __getattr__(self, name):
            calls = self._calls
            def call(*a, **k):
                ri = _RecInstr()
                calls.append((name, a, k, ri))
                return ri
            return call

    _recs = {n: [] for n in ("sync", "gpsimd", "tensor", "vector", "scalar")}

    def on(engname):
        def deco(fn):
            fn(_RecEngine(_recs[engname]))
        return deco

    def flush_program():
        for engname, calls in _recs.items():
            if not calls:
                continue
            def body(eng, calls=calls):
                for name, a, k, ri in calls:
                    instr = getattr(eng, name)(*a, **k)
                    for sem, n in ri.incs:
                        instr.then_inc(sem, n)
            getattr(blk, engname)(body)

    def vsep(v):
        v.tensor_copy(sepb[:], sepb[:])

    def fin():
        @on("gpsimd")
        def _(g):
            wt(g, *dma_sems)
            wt(g, "CC")
        flush_program()
        est.close()
        return nc

    def tap(name, src, *waits):
        if not DEBUG:
            return

        @on("gpsimd")
        def _(g):
            wt(g, *waits)
            inc(g.dma_start(out=taps[name], in_=src), "KG", 16)
        for en in ("vector", "scalar", "tensor"):
            @on(en)
            def _(e):
                wt(e, "KG")

    wseq = []
    for _s in range(S):
        wseq += ["wq1t", "wk1t", "wq2t", "wk2t", "wvt", "wgt", "wpt"]
    arena_next = [0]
    arena_ready = {}

    def arena_load_next():
        e = arena_next[0]
        if e >= len(wseq):
            return
        if NO_ARENA:
            arena_next[0] += 1
            arena_ready[e] = 0
            return
        arena_next[0] += 1
        slot, name = e % 2, wseq[e]

        @on("gpsimd")
        def _(g):
            use(g, f"arena{slot}")
            src = wdram[name].rearrange("(k p) n -> p k n", p=128)
            i = g.dma_start(out=arena[slot].ap().rearrange("p (k n) -> p k n", k=CH), in_=src)
            inc(i, f"A{slot}", 16)
            arena_ready[e] = cnt[f"A{slot}"]

    def arena_wait(eng, e):
        eng.wait_ge(sems[f"A{e % 2}"], arena_ready[e])

    def arena_release(e, sem):
        release(f"arena{e % 2}", sem)

    @on("sync")
    def _(sy):
        pairs = [(ident_bf, ident_bf_d), (ident_f, ident_f_d), (mask_f, mask_f_d),
                 (ipat_f, ipat_f_d), (ones2, ones2_d), (onesr_bf, onesr_bf_d),
                 (onesr_f, onesr_f_d), (bias_rep, bias_rep_d), (ws_rep, ws_rep_d),
                 (kcol, kcol_d)]
        pairs += [(bcols[n], bias_d[n]) for n in bcols]
        for dst, src in pairs:
            inc(sy.dma_start(out=dst[:], in_=src), "KC", 16)

    @on("gpsimd")
    def _(g):
        for bi, n in enumerate(brows):
            inc(g.dma_start(out=brows[n][:], in_=row_d[n]), f"KB{bi}", 16)

    @on("vector")
    def _(v):
        wt(v, "KC")
        v.memset(sepb[:], 0.0)
        inc(v.memset(rmax[:], 0.0), "VE")

    @on("tensor")
    def _(t):
        wt(t, "KC", "KB0", "KB1", "KB2")

    arena_load_next()
    arena_load_next()

    def proj_fm(eidx, src_wide, dst_wide, bias_cols, entry=None):
        if entry is None:
            entry = [("SE", cnt["SE"]), ("VE", cnt["VE"])]
        rot = [(pP[:, 0:512], "pP"), (pT[:, 0:512], "pT"),
               (pBig[:, 0:512], "pBigA"), (pBig[:, 512:1024], "pBigB"),
               (pBig2[:, 0:512], "pBig2A"), (pBig2[:, 512:1024], "pBig2B")]
        for m in range(CH):
            for sl in range(4):
                pp, pn = rot[(m * 4 + sl) % 6]

                first = (m == 0 and sl == 0)

                @on("tensor")
                def _(t, m=m, sl=sl, pp=pp, pn=pn, first=first):
                    if first:
                        arena_wait(t, eidx)
                        for sm_, c_ in entry:
                            t.wait_ge(sems[sm_], c_)
                    use(t, pn)
                    i = None
                    for k in range(CH):
                        i = t.matmul(pp,
                                     arena[eidx % 2].ap()[:, k * D + m * 128:k * D + (m + 1) * 128],
                                     src_wide[:, k * T + sl * 512:k * T + (sl + 1) * 512],
                                     start=(k == 0), stop=(k == CH - 1))
                    inc(i, "TE")

                @on("scalar")
                def _(s, m=m, sl=sl, pp=pp, pn=pn):
                    wt(s, "TE")
                    i = s.activation(dst_wide[:, m * T + sl * 512:m * T + (sl + 1) * 512],
                                     pp, AF.Gelu, bias=bias_cols[:, m:m + 1])
                    inc(i, "SE")
                    release(pn, "SE")
        arena_release(eidx, "TE")

    slot_sem = {"xt0": "X0", "xt1": "X1", "lnsA": "XL"}

    def ln1_stats_unit(sq, w, slot_ap, slot_name):
        xpv = xin.rearrange("(s u r) d -> s r u d", s=S, r=16)
        sm = slot_sem[slot_name]

        @on("sync")
        def _(sy):
            use(sy, slot_name)
            inc(sy.dma_start(out=slot_ap, in_=xpv[sq, w]), sm, 16)

        @on("scalar")
        def _(sc):
            wt(sc, sm)
            sc.activation(scrf[:], slot_ap, AF.Copy, accum_out=stats[:, w:w + 1])
            i = sc.activation(scrf[:], slot_ap, AF.Square, accum_out=stats[:, 16 + w:17 + w])
            inc(i, "SE")
            release(slot_name, "SE")

    A_sb_tile = A_sb.ap()[:, 0:D]
    for s_ in range(S):
        eb = 7 * s_
        xperm = xin.rearrange("(s u r) d -> s r u d", s=S, r=16)
        xnat = xin.rearrange("(s w p) d -> s w p d", s=S, p=128)

        if s_ == 0:
            for w in range(W):
                ln1_stats_unit(s_, w, xt[w % 2][:], f"xt{w % 2}")

        @on("vector")
        def _(v):
            wt(v, "SE")
            v.tensor_scalar(stats[:, 32:48], stats[:, 0:16], 1.0 / D, None, ALU.mult)
            v.tensor_scalar(stats[:, 48:64], stats[:, 16:32], 1.0 / D, None, ALU.mult)
            v.tensor_tensor(stats[:, 0:16], stats[:, 32:48], stats[:, 32:48], ALU.mult)
            vsep(v)
            i = v.tensor_tensor(stats[:, 16:32], stats[:, 48:64], stats[:, 0:16], ALU.subtract)
            inc(i, "VE")

        @on("scalar")
        def _(s):
            wt(s, "VE")
            s.activation(stats[:, 0:16], stats[:, 16:32], AF.Ln, bias=kcol[:, 0:1])
            i = s.activation(stats[:, 48:64], stats[:, 0:16], AF.Exp, scale=kcol[:, 1:2])
            inc(i, "SE")

        @on("vector")
        def _(v):
            wt(v, "SE")
            v.scalar_tensor_tensor(stats[:, 64:80], stats[:, 32:48], -1.0, stats[:, 48:64],
                                   ALU.mult, ALU.mult)
            vsep(v)
            i = v.tensor_copy(sepb[:], sepb[:])
            inc(i, "VE")

        if TRUNC == 1 and s_ == 0:
            tap("tap_stats", stats[:], "VE", "SE")
            return fin()
        for w in range(W):
            @on("sync")
            def _(sy, w=w):
                use(sy, f"xt{w % 2}")
                inc(sy.dma_start(out=xt[w % 2][:], in_=xperm[s_, w]), f"X{w % 2}", 16)

            @on("vector")
            def _(v, w=w):
                wt(v, f"X{w % 2}")
                use(v, f"xnt{w % 2}")
                i = v.tensor_scalar(xnt[w % 2][:], xt[w % 2][:],
                                    stats[:, 48 + w:49 + w], stats[:, 64 + w:65 + w],
                                    ALU.mult, ALU.add)
                inc(i, "VE")
                release(f"xt{w % 2}", "VE")

            for g2 in range(2):
                rot4 = [(pP[:, 0:512], "pP"), (pT[:, 0:512], "pT"),
                        (pBig[:, 0:512], "pBigA"), (pBig[:, 512:1024], "pBigB")]
                pp, pn = rot4[(w * 2 + g2) % 4]

                @on("tensor")
                def _(t, w=w, g2=g2, pp=pp, pn=pn):
                    wt(t, "VE")
                    use(t, pn)
                    i = None
                    for mm_ in range(4):
                        m = g2 * 4 + mm_
                        i = t.matmul(pp[:, mm_ * 128:(mm_ + 1) * 128],
                                     xnt[w % 2][:, m * 128:(m + 1) * 128], ident_bf[:],
                                     start=(mm_ == 0), stop=(mm_ == 3), skip_group_check=True)
                    inc(i, "TE")

                @on("vector")
                def _(v, w=w, g2=g2, pp=pp, pn=pn):
                    wt(v, "TE")
                    use(v, "bufA")
                    dst = bufA.ap().rearrange("p (c t) -> p c t", c=CH)[:, g2 * 4:(g2 + 1) * 4,
                                                                       w * 128:(w + 1) * 128]
                    i = v.tensor_copy(dst, pp.rearrange("p (c u) -> p c u", c=4))
                    inc(i, "VE")
                    release(pn, "VE")
                    if g2 == 1:
                        release(f"xnt{w % 2}", "VE")

        if s_ == 0:
            tap("tap_xn", bufA.ap()[:], "VE")
            tap("tap_stats", stats[:], "VE", "SE")
        if TRUNC == 2:
            return fin()
        proj_fm(eb + 0, bufA.ap(), bufB.ap(), bcols["qb1c"])
        arena_load_next()
        proj_fm(eb + 1, bufA.ap(), bufC.ap(), bcols["kb1c"])
        arena_load_next()
        if s_ == 0:
            tap("tap_q1", bufB.ap()[:], "SE")
            tap("tap_k1", bufC.ap()[:], "SE")
        if TRUNC == 3:
            return fin()

        sp_done = {}

        def a_mms(w):
            @on("tensor")
            def _(t, w=w):
                t.wait_ge(sems["SE"], sp_done[w])
                i = None
                for j in range(CH):
                    i = t.matmul(pA[:, j * 128:(j + 1) * 128], ktm[w % 2][:, j * 128:(j + 1) * 128],
                                 qtm[w % 2][:, j * 128:(j + 1) * 128],
                                 start=(w == 0 and j in (0, 4)), stop=(w == W - 1),
                                 skip_group_check=True)
                inc(i, "TE")
                release(f"qtm{w % 2}", "TE")
                release(f"ktm{w % 2}", "TE")

        for w in range(W):
            for qk in range(2):
                src_wide = bufB.ap() if qk == 0 else bufC.ap()
                eidx = eb + 2 + qk
                brow = brows["b2q"] if qk == 0 else brows["b2k"]
                slot = (2 * w + qk) % 3
                if slot == 0:
                    halves, pnames = [pBig[:, 0:512], pBig[:, 512:1024]], ["pBigA", "pBigB"]
                elif slot == 1:
                    halves, pnames = [pBig2[:, 0:512], pBig2[:, 512:1024]], ["pBig2A", "pBig2B"]
                else:
                    halves, pnames = [pP[:, 0:512], pT[:, 0:512]], ["pP", "pT"]
                dsttm = qtm if qk == 0 else ktm
                dname = ("qtm" if qk == 0 else "ktm") + str(w % 2)
                first = (w == 0 and qk == 0)

                @on("tensor")
                def _(t, w=w, src_wide=src_wide, eidx=eidx, brow=brow, halves=halves, pnames=pnames, first=first):
                    if w == 0:
                        arena_wait(t, eidx)
                    if first:
                        wt(t, "SE")
                    use_l(t, pnames)
                    for k in range(CH):
                        for hf in range(2):
                            t.matmul(halves[hf],
                                     src_wide[:, k * T + w * 128:k * T + (w + 1) * 128],
                                     arena[eidx % 2].ap()[:, k * D + hf * 512:k * D + (hf + 1) * 512],
                                     start=(k == 0), stop=False)
                    i = None
                    for hf in range(2):
                        i = t.matmul(halves[hf], onesr_bf[:],
                                     brow[:, hf * 512:(hf + 1) * 512], start=False, stop=True)
                    inc(i, "TE")

                @on("scalar")
                def _(s, w=w, halves=halves, pnames=pnames, dsttm=dsttm, dname=dname, qk=qk, slot=slot):
                    wt(s, "TE")
                    use(s, dname)
                    if slot == 2:
                        s.activation(scrf[:, 0:512], halves[0], AF.Exp)
                        s.activation(scrf[:, 512:1024], halves[1], AF.Exp)
                    else:
                        s.activation(scrf[:], halves[0].tensor.ap() if False else (pBig[:] if slot == 0 else pBig2[:]), AF.Exp)
                    i = s.activation(dsttm[w % 2][:], scrf[:], AF.Ln, bias=1.0)
                    inc(i, "SE")
                    for pn in pnames:
                        release(pn, "SE")
                    if qk == 1:
                        sp_done[w] = cnt["SE"]

            if w > 0:
                a_mms(w - 1)
            if s_ == 0 and w == 0:
                tap("tap_qtm", qtm[0].ap()[:], "SE")
                tap("tap_ktm", ktm[0].ap()[:], "SE")
        a_mms(W - 1)
        arena_release(eb + 2, "TE")
        arena_release(eb + 3, "TE")
        arena_load_next()
        arena_load_next()

        @on("vector")
        def _(v):
            wt(v, "TE")
            v.tensor_copy(A_sb[:], pA[:])
            i = None
            for j in range(CH):
                sl = slice(j * 128, (j + 1) * 128)
                v.tensor_tensor(scrf[:, sl], A_sb[:, sl], mask_f[:], ALU.mult)
                i = v.tensor_tensor(A2bf.ap()[:, sl], scrf[:, sl], ipat_f[:], ALU.add)
            inc(i, "VE")

        @on("tensor")
        def _(t):
            wt(t, "VE")
            use(t, "pT")
            i = None
            for j in range(CH):
                i = t.matmul(pT[:, 2 * j:2 * j + 2], A_sb[:, j * 128:(j + 1) * 128], ones2[:],
                             start=(j == 0), stop=(j == CH - 1), skip_group_check=True)
            inc(i, "TE")

        @on("vector")
        def _(v):
            wt(v, "TE")
            release("lnsA", "VE")
            zv = pT[:, 0:16].rearrange("p (j two) -> p two j", two=2)
            v.tensor_copy(zz[0:64, :], zv[0:64, 0, :])
            v.tensor_copy(zz[64:128, :], zv[64:128, 1, :])
            vsep(v)
            v.tensor_scalar(zz[:], zz[:], SCALE, float(D), ALU.mult, ALU.add)
            vsep(v)
            i = v.reciprocal(zz[:], zz[:])
            inc(i, "VE")
            release("pT", "VE")

        if s_ == 0:
            tap("tap_Asb", A_sb[:], "VE")
            tap("tap_A2", A2bf[:], "VE")
            tap("tap_zz", zz[:], "VE")
        if TRUNC == 4:
            return fin()
        proj_fm(eb + 4, bufA.ap(), bufB.ap(), bcols["vbc"])
        arena_load_next()
        proj_fm(eb + 5, bufA.ap(), bufC.ap(), bcols["gbc"])
        arena_load_next()

        rotB = [(pP[:, 0:512], "pP"), (pT[:, 0:512], "pT"),
                (pBig[:, 0:512], "pBigA"), (pBig[:, 512:1024], "pBigB")]
        for sl in range(4):
            for j in range(CH):
                pp, pn = rotB[(sl * CH + j) % 4]

                first = (j == 0 and sl == 0)

                @on("tensor")
                def _(t, j=j, sl=sl, pp=pp, pn=pn, first=first):
                    if first:
                        wt(t, "VE", "SE")
                    use(t, pn)
                    i = t.matmul(pp, A2bf.ap()[:, j * 128:(j + 1) * 128],
                                 bufB.ap()[:, j * T + sl * 512:j * T + (sl + 1) * 512],
                                 start=True, stop=True)
                    inc(i, "TE")

                @on("vector")
                def _(v, j=j, sl=sl, pp=pp, pn=pn):
                    wt(v, "TE")
                    i = v.scalar_tensor_tensor(
                        bufA.ap()[:, j * T + sl * 512:j * T + (sl + 1) * 512],
                        pp, zz[:, j:j + 1],
                        bufC.ap()[:, j * T + sl * 512:j * T + (sl + 1) * 512],
                        ALU.mult, ALU.mult)
                    inc(i, "VE")
                    release(pn, "VE")

            @on("sync")
            def _(sy, sl=sl):
                wt(sy, "VE")
                if sl == 0:
                    use(sy, "bufD")
                for jj in (2 * sl, 2 * sl + 1):
                    for h2 in range(2):
                        for r2 in range(2):
                            src = bufA.ap()[h2 * 64:(h2 + 1) * 64, :].rearrange(
                                "p (hp c) -> p hp c", hp=CH)[:, :, (2 * jj + r2) * 128:(2 * jj + r2 + 1) * 128]
                            dst = bufD.ap()[r2 * 64:(r2 + 1) * 64, jj * T:(jj + 1) * T].rearrange(
                                "p (hp h2x u) -> p hp h2x u", hp=CH, h2x=2)[:, :, h2, :]
                            inc(sy.dma_start(out=dst, in_=src), "SC", 16)

        if s_ == 0:
            tap("tap_v", bufB.ap()[:], "SE")
            tap("tap_g", bufC.ap()[:], "SE")
            tap("tap_E", bufA.ap()[:], "VE")

        if s_ == 0:
            tap("tap_Es", bufD.ap()[:], "SC")
        if TRUNC == 5:
            return fin()
        for w in range(W):
            @on("sync")
            def _(sy, w=w):
                use(sy, f"xt{w % 2}")
                inc(sy.dma_start(out=xt[w % 2][:], in_=xnat[s_, w]), f"X{w % 2}", 16)

            ppm, ppmn = (pBig, ["pBigA", "pBigB"]) if w % 2 == 0 else (pBig2, ["pBig2A", "pBig2B"])

            @on("tensor")
            def _(t, w=w, ppm=ppm, ppmn=ppmn):
                if w == 0:
                    arena_wait(t, eb + 6)
                    wt(t, "SC")
                wt(t, f"X{w % 2}")
                use_l(t, ppmn)
                for k in range(CH):
                    for hf in range(2):
                        t.matmul(ppm[:, hf * 512:(hf + 1) * 512],
                                 bufD.ap()[:, k * T + w * 128:k * T + (w + 1) * 128],
                                 arena[(eb + 6) % 2].ap()[:, k * D + hf * 512:k * D + (hf + 1) * 512],
                                 start=(k == 0), stop=False)
                i = None
                for hf in range(2):
                    i = t.matmul(ppm[:, hf * 512:(hf + 1) * 512], onesr_bf[:],
                                 brows["bp"][:, hf * 512:(hf + 1) * 512], start=False, stop=True)
                inc(i, "TE")

            @on("vector")
            def _(v, w=w, ppm=ppm, ppmn=ppmn):
                wt(v, "TE")
                use(v, f"xp{w % 2}")
                i = v.tensor_tensor(xp_t[w % 2], ppm[:], xt[w % 2][:], ALU.add)
                inc(i, "VE")
                rel_l(ppmn, "VE")
                release(f"xt{w % 2}", "VE")

            @on("scalar")
            def _(s, w=w):
                wt(s, "VE")
                s.activation(scrf[:], xp_t[w % 2], AF.Copy, accum_out=st2[:, 0:1])
                i = s.activation(scrf[:], xp_t[w % 2], AF.Square, accum_out=st2[:, 1:2])
                inc(i, "SE")

            @on("vector")
            def _(v, w=w):
                wt(v, "SE")
                v.tensor_scalar(st2[:, 2:3], st2[:, 0:1], 1.0 / D, None, ALU.mult)
                v.tensor_scalar(st2[:, 3:4], st2[:, 1:2], 1.0 / D, None, ALU.mult)
                v.tensor_tensor(st2[:, 4:5], st2[:, 2:3], st2[:, 2:3], ALU.mult)
                vsep(v)
                i = v.tensor_tensor(st2[:, 4:5], st2[:, 3:4], st2[:, 4:5], ALU.subtract)
                inc(i, "VE")

            @on("scalar")
            def _(s, w=w):
                wt(s, "VE")
                s.activation(st2[:, 5:6], st2[:, 4:5], AF.Ln, bias=kcol[:, 0:1])
                i = s.activation(st2[:, 5:6], st2[:, 5:6], AF.Exp, scale=kcol[:, 1:2])
                inc(i, "SE")

            @on("vector")
            def _(v, w=w):
                wt(v, "SE")
                v.scalar_tensor_tensor(st2[:, 6:7], st2[:, 2:3], -1.0, st2[:, 5:6],
                                       ALU.mult, ALU.mult)
                vsep(v)
                i = v.tensor_copy(sepb[:], sepb[:])
                inc(i, "VE")

            @on("vector")
            def _(v, w=w):
                use(v, f"xn2{w % 2}", f"xb{w % 2}")
                v.tensor_scalar(xn2_t[w % 2], xp_t[w % 2], st2[:, 5:6], st2[:, 6:7],
                                ALU.mult, ALU.add)
                v.reduce_max(st2[:, 7:8], xn2_t[w % 2], axis=AX.X, apply_absolute_value=True)
                vsep(v)
                i = v.tensor_tensor(rmax[:], rmax[:], st2[:, 7:8], ALU.max)
                inc(i, "VE")

            if s_ == 0 and w == 0:
                tap("tap_xp", xp_t[0], "VE")

            @on("gpsimd")
            def _(g, w=w):
                wt(g, "VE")
                use(g, f"xb{w % 2}")
                i = g.tensor_tensor(xb_t[w % 2], xp_t[w % 2], bias_rep[:], ALU.add)
                inc(i, "GE", 1)
                release(f"xp{w % 2}", "GE")
                row = s_ * T + w * 128
                inc(g.dma_start(out=scr_xn2[row:row + 128, :], in_=xn2_t[w % 2]), f"N{w % 2}", 16)
                release(f"xn2{w % 2}", f"N{w % 2}")
                inc(g.dma_start(out=scr_xb[row:row + 128, :], in_=xb_t[w % 2]), f"M{w % 2}", 16)
                release(f"xb{w % 2}", f"M{w % 2}")
                if w == W - 1:
                    release_multi("bufA", ["N0", "N1", "M0", "M1"])

            if s_ + 1 < S:
                ln1_stats_unit(s_ + 1, w, A_sb_tile, "lnsA")

        arena_release(eb + 6, "TE")
        arena_load_next()
        release("bufD", "TE")

    pre_cc = {n: cnt[n] for n in ["N0", "N1", "M0", "M1", "TE", "VE", "SE"]}

    @on("gpsimd")
    def _(g):
        wt(g, "VE", "SE", "TE")
        src = fwt.rearrange("(k p) n -> p k n", p=128)
        inc(g.dma_start(out=fwi8.rearrange("p (k n) -> p k n", k=CH), in_=src), "KF", 16)

    @on("vector")
    def _(v):
        wt(v, "KF")
        inc(v.tensor_copy(arena[0].ap(), fwi8), "VE")

    @on("tensor")
    def _(t):
        wt(t, "VE")
        use(t, "pT")
        i = t.matmul(pT[0:1, 0:128], rmax[:], ident_f[:], start=True, stop=True)
        inc(i, "TE")

    @on("vector")
    def _(v):
        wt(v, "TE")
        inc(v.reduce_max(gtiny[0:1, 0:1], pT[0:1, 0:128], axis=AX.X), "VE")

    @on("gpsimd")
    def _(g):
        wt(g, "VE")
        inc(g.dma_start(out=cc_in.ap(), in_=gtiny[0:1, 0:1]), "KG", 16)
        wt(g, "KG")
        inc(g.collective_compute("AllGather", ALU.bypass, replica_groups=[list(range(8))],
                                 ins=[cc_in.ap().opt()], outs=[cc_out.ap().opt()]), "CC", 1)
        wt(g, "CC")
        inc(g.dma_start(out=gtiny[0:1, 1:9], in_=cc_out.ap()), "KG", 16)

    @on("vector")
    def _(v):
        wt(v, "KG")
        v.reduce_max(gtiny[0:1, 9:10], gtiny[0:1, 1:9], axis=AX.X)
        vsep(v)
        v.tensor_scalar(gtiny[0:1, 10:11], gtiny[0:1, 9:10], 1.0 / 127.0, EPS, ALU.mult, ALU.max)
        vsep(v)
        inc(v.reciprocal(gtiny[0:1, 11:12], gtiny[0:1, 10:11]), "VE")

    @on("tensor")
    def _(t):
        wt(t, "VE")
        i = t.matmul(pT[:, 0:2], onesr_f[:], gtiny[0:1, 10:12], start=True, stop=True)
        inc(i, "TE")

    @on("vector")
    def _(v):
        wt(v, "TE")
        v.tensor_copy(xs_col[:], pT[:, 0:1])
        v.tensor_copy(xinv_col[:], pT[:, 1:2])
        vsep(v)
        i = v.tensor_scalar(wsx, ws_rep[:], xs_col[:], None, ALU.mult)
        inc(i, "VE")
        release("pT", "VE")

    fw_bf = arena[0].ap()

    xs_ready = ("VE", cnt["VE"])

    @on("sync")
    def _(sy):
        for n in ("N0", "N1", "M0", "M1"):
            sy.wait_ge(sems[n], pre_cc[n])
        sy.wait_ge(sems["TE"], pre_cc["TE"])
        use(sy, "bufD")

    @on("scalar")
    def _(sc):
        sc.wait_ge(sems["VE"], pre_cc["VE"])
        sc.wait_ge(sems["TE"], pre_cc["TE"])
        for n in ("M0", "M1"):
            sc.wait_ge(sems[n], pre_cc[n])

    for t_ in range(S * W // 2):
        row = t_ * 256
        pg = t_ % 2

        @on("sync")
        def _(sy, row=row, pg=pg):
            use(sy, f"cln2{pg}")
            src = scr_xn2.rearrange("(n two p) d -> n p two d", two=2, p=128)[row // 256]
            inc(sy.dma_start(out=c_ln2[pg].rearrange("p (two d) -> p two d", two=2), in_=src), f"L{pg}", 16)

        @on("scalar")
        def _(sc, row=row, pg=pg):
            use(sc, f"cxb{pg}")
            src = scr_xb.rearrange("(n two p) d -> n p two d", two=2, p=128)[row // 256]
            inc(sc.dma_start(out=c_xb[pg].rearrange("p (two d) -> p two d", two=2), in_=src), f"B{pg}", 16)

        @on("scalar")
        def _(sc, pg=pg):
            wt(sc, f"L{pg}", f"B{pg}")
            sc.wait_ge(sems[xs_ready[0]], xs_ready[1])
            i = sc.activation(c_ln2[pg], c_ln2[pg], AF.Identity, scale=xinv_col[:], bias=kcol[:, 2:3])
            inc(i, "SE")

        @on("vector")
        def _(v, pg=pg):
            wt(v, "SE")
            use(v, f"cxq{pg}")
            i = v.tensor_scalar(c_xq[pg], c_ln2[pg], MAGIC, None, ALU.subtract)
            inc(i, "VE")
            release(f"cln2{pg}", "VE")

        rotC = [(pP, "pP"), (pT, "pT"), (pA[:, 0:512], "pA0"), (pA[:, 512:1024], "pA1")]
        for g2 in range(4):
            pp, pn = rotC[g2]

            @on("tensor")
            def _(t, pg=pg, g2=g2, pp=pp, pn=pn):
                if g2 == 0:
                    wt(t, "VE")
                use(t, pn)
                i = None
                for mm_ in range(4):
                    m = g2 * 4 + mm_
                    i = t.matmul(pp[:, mm_ * 128:(mm_ + 1) * 128],
                                 c_xq[pg][:, m * 128:(m + 1) * 128], ident_bf[:],
                                 start=(mm_ == 0), stop=(mm_ == 3), skip_group_check=True)
                inc(i, "TE")

            @on("scalar")
            def _(sc, pg=pg, g2=g2, pp=pp, pn=pn):
                wt(sc, "TE")
                if g2 == 0:
                    use(sc, f"cxqf{pg}")
                i = sc.activation(c_xqf[pg][:, g2 * 512:(g2 + 1) * 512], pp[:, 0:512], AF.Copy)
                inc(i, "SE")
                release(pn, "SE")
                if g2 == 3:
                    release(f"cxq{pg}", "SE")

        for tau in range(2):
            pfm, pfn = (pBig, ["pBigA", "pBigB"]) if tau == 0 else (pBig2, ["pBig2A", "pBig2B"])

            @on("tensor")
            def _(t, pg=pg, tau=tau, pfm=pfm, pfn=pfn):
                if tau == 0:
                    wt(t, "SE")
                use_l(t, pfn)
                i = None
                for k in range(CH):
                    for hf in range(2):
                        i = t.matmul(pfm[:, hf * 512:(hf + 1) * 512],
                                     c_xqf[pg][:, tau * 1024 + k * 128:tau * 1024 + (k + 1) * 128],
                                     fw_bf[:, k * D + hf * 512:k * D + (hf + 1) * 512],
                                     start=(k == 0), stop=(k == CH - 1))
                inc(i, "TE")
                if tau == 1:
                    release(f"cxqf{pg}", "TE")

            @on("vector")
            def _(v, pg=pg, tau=tau, pfm=pfm, pfn=pfn):
                wt(v, "TE")
                use(v, f"cscr{tau}")
                i = v.tensor_tensor(cscr2[:, tau * D:(tau + 1) * D], pfm[:], wsx, ALU.mult)
                inc(i, "VE")
                rel_l(pfn, "VE")

            if tau == 0:
                @on("vector")
                def _(v, pg=pg, tau=tau):
                    use(v, f"cout{pg}")
                    i = v.tensor_tensor(c_out[pg][:, tau * D:(tau + 1) * D],
                                        cscr2[:, tau * D:(tau + 1) * D], c_xb[pg][:, tau * D:(tau + 1) * D],
                                        ALU.add)
                    inc(i, "VE")
                    release(f"cscr{tau}", "VE")
            else:
                @on("gpsimd")
                def _(g, pg=pg, tau=tau):
                    wt(g, "VE")
                    i = g.tensor_tensor(c_out[pg][:, tau * D:(tau + 1) * D],
                                        cscr2[:, tau * D:(tau + 1) * D], c_xb[pg][:, tau * D:(tau + 1) * D],
                                        ALU.add)
                    inc(i, "GE", 1)
                    release(f"cscr{tau}", "GE")
                    release(f"cxb{pg}", "GE")

        @on("gpsimd")
        def _(g, row=row, pg=pg):
            wt(g, "VE")
            dst = out_d.rearrange("(n two p) d -> n p two d", two=2, p=128)[row // 256]
            inc(g.dma_start(out=dst, in_=c_out[pg].rearrange("p (two d) -> p two d", two=2)), f"O{pg}", 16)
            release(f"cout{pg}", f"O{pg}")

    @on("gpsimd")
    def _(g):
        wt(g, "O0", "O1")

    flush_program()
    est.close()
    return nc


def kernel(**inputs):
    x = np.ascontiguousarray(np.asarray(inputs["L0_attn_x"]), dtype=np.float32)
    g = lambda n: np.asarray(inputs[n])
    wT = lambda n: np.ascontiguousarray(g(n).astype(np.float32).T)
    bc = lambda n: np.ascontiguousarray(g(n).astype(np.float32).reshape(CH, 128).T)

    m128 = np.kron(np.eye(2, dtype=np.float32), np.ones((64, 64), np.float32)) * np.float32(SCALE)
    i128 = np.eye(128, dtype=np.float32)
    common = dict(
        wq1t=wT("L0_attn_q_w1"), wk1t=wT("L0_attn_k_w1"),
        wq2t=wT("L0_attn_q_w2"), wk2t=wT("L0_attn_k_w2"),
        wvt=wT("L0_attn_v_w"), wgt=wT("L0_attn_g_w"), wpt=wT("L0_attn_p_w"),
        fwt=np.ascontiguousarray(g("L0_ffn_weight_int8").T),
        qb1c=bc("L0_attn_q_b1"), kb1c=bc("L0_attn_k_b1"),
        vbc=bc("L0_attn_v_b"), gbc=bc("L0_attn_g_b"),
        b2q=g("L0_attn_q_b2").reshape(1, D).astype(np.float32),
        b2k=g("L0_attn_k_b2").reshape(1, D).astype(np.float32),
        bp=g("L0_attn_p_b").reshape(1, D).astype(np.float32),
        bias_rep=np.ascontiguousarray(np.broadcast_to(g("L0_ffn_bias").reshape(1, D).astype(np.float32), (128, D))),
        ws_rep=np.ascontiguousarray(np.broadcast_to(g("L0_ffn_weight_scale").reshape(1, D).astype(np.float32), (128, D))),
        ident_bf=np.eye(128, dtype=BF),
        ident_f=i128,
        mask_f=np.ascontiguousarray(m128),
        ipat_f=i128,
        ones2=np.concatenate([np.repeat([[1.0, 0.0]], 64, 0), np.repeat([[0.0, 1.0]], 64, 0)]).astype(np.float32),
        onesr_bf=np.ones((1, 128), dtype=BF),
        onesr_f=np.ones((1, 128), dtype=np.float32),
        kcol=np.ascontiguousarray(np.broadcast_to(np.array([[EPS, -0.5, MAGIC, 0.0]], np.float32), (128, 4))),
    )

    nc = build_nc()
    in_maps = []
    for c in range(8):
        mm = dict(common)
        mm["xin"] = np.ascontiguousarray(x[4 * c:4 * c + 4].reshape(S * T, D))
        in_maps.append(mm)
    res = run_bass_kernel_spmd(nc, in_maps, core_ids=list(range(8)))
    return np.concatenate([res.results[c]["out"].reshape(4, T, D) for c in range(8)], axis=0)


if __name__ == "__main__":
    nc = build_nc()
    print("built ok; instructions:", len(nc.inst_names()) if hasattr(nc, "inst_names") else "?")


# version 25 (final)
# speedup vs baseline: 1.0226x; 1.0226x over previous
import contextlib
import numpy as np
import ml_dtypes

import concourse.bass as bass
import concourse.mybir as mybir
from concourse.bass_utils import run_bass_kernel_spmd

F32 = mybir.dt.float32
BF16 = mybir.dt.bfloat16
I8 = mybir.dt.int8
AF = mybir.ActivationFunctionType
ALU = mybir.AluOpType
AX = mybir.AxisListType
BF = ml_dtypes.bfloat16

DEBUG = False
NO_ARENA = False
TRUNC = 0
S, T, D, H, HD = 4, 2048, 1024, 16, 64
W = T // 128
CH = D // 128
SCALE = HD ** -0.5
EPS = 1e-5
MAGIC = float(3 * 2 ** 22)


def build_nc():
    nc = bass.Bass("TRN2", target_bir_lowering=False, detect_race_conditions=False)

    xin = nc.dram_tensor("xin", [S * T, D], F32, kind="ExternalInput").ap()
    wnames = ["wq1t", "wk1t", "wq2t", "wk2t", "wvt", "wgt", "wpt"]
    wdram = {n: nc.dram_tensor(n, [D, D], F32, kind="ExternalInput").ap() for n in wnames}
    fwt = nc.dram_tensor("fwt", [D, D], I8, kind="ExternalInput").ap()
    bias_d = {n: nc.dram_tensor(n, [128, CH], F32, kind="ExternalInput").ap()
              for n in ["qb1c", "kb1c", "vbc", "gbc"]}
    row_d = {n: nc.dram_tensor(n, [1, D], F32, kind="ExternalInput").ap()
             for n in ["b2q", "b2k", "bp"]}
    bias_rep_d = nc.dram_tensor("bias_rep", [128, D], F32, kind="ExternalInput").ap()
    ws_rep_d = nc.dram_tensor("ws_rep", [128, D], F32, kind="ExternalInput").ap()
    ident_bf_d = nc.dram_tensor("ident_bf", [128, 128], BF16, kind="ExternalInput").ap()
    ident_f_d = nc.dram_tensor("ident_f", [128, 128], F32, kind="ExternalInput").ap()
    mask_f_d = nc.dram_tensor("mask_f", [128, 128], F32, kind="ExternalInput").ap()
    ipat_f_d = nc.dram_tensor("ipat_f", [128, 128], F32, kind="ExternalInput").ap()
    ones2_d = nc.dram_tensor("ones2", [128, 2], F32, kind="ExternalInput").ap()
    onesr_bf_d = nc.dram_tensor("onesr_bf", [1, 128], BF16, kind="ExternalInput").ap()
    onesr_f_d = nc.dram_tensor("onesr_f", [1, 128], F32, kind="ExternalInput").ap()
    kcol_d = nc.dram_tensor("kcol", [128, 4], F32, kind="ExternalInput").ap()

    out_d = nc.dram_tensor("out", [S * T, D], F32, kind="ExternalOutput").ap()
    scr_xn2 = nc.dram_tensor("scr_xn2", [S * T, D], F32).ap()
    scr_xb = nc.dram_tensor("scr_xb", [S * T, D], F32).ap()
    cc_in = nc.dram_tensor("cc_in", [1, 1], F32)
    cc_out = nc.dram_tensor("cc_out", [8, 1], F32)
    taps = {}
    if DEBUG:
        for tn, shp, dt in [("tap_xn", [128, CH * T], BF16), ("tap_q1", [128, CH * T], BF16),
                            ("tap_k1", [128, CH * T], BF16), ("tap_Asb", [128, D], F32),
                            ("tap_A2", [128, D], BF16), ("tap_zz", [128, 8], F32),
                            ("tap_v", [128, CH * T], BF16), ("tap_g", [128, CH * T], BF16),
                            ("tap_E", [128, CH * T], BF16), ("tap_Es", [128, CH * T], BF16),
                            ("tap_stats", [128, 80], F32), ("tap_xp", [128, D], F32),
                            ("tap_qtm", [128, D], BF16), ("tap_ktm", [128, D], BF16),
                            ("tap_sxn2", [S * T, D], F32), ("tap_sxb", [S * T, D], F32)]:
            taps[tn] = nc.dram_tensor(tn, shp, dt, kind="ExternalOutput").ap()

    est = contextlib.ExitStack()
    sbt = lambda name, shape, dt: est.enter_context(nc.sbuf_tensor(name, shape, dt))
    pst = lambda name, shape: est.enter_context(nc.psum_tensor(name, shape, F32))

    bufA = sbt("bufA", [128, CH * T], BF16)
    bufB = sbt("bufB", [128, CH * T], BF16)
    bufC = sbt("bufC", [128, CH * T], BF16)
    bufD = sbt("bufD", [128, CH * T], BF16)
    arena = [sbt(f"wslot{i}", [128, CH * D], BF16) for i in range(2)]

    xt = [sbt(f"xt{i}", [128, D], F32) for i in range(2)]
    qtm = [sbt(f"qtm{i}", [128, D], BF16) for i in range(2)]
    ktm = [sbt(f"ktm{i}", [128, D], BF16) for i in range(2)]
    xnt = [sbt(f"xnt{i}", [128, D], BF16) for i in range(2)]
    scrf = sbt("scrf", [128, D], F32)
    A_sb = sbt("A_sb", [128, D], F32)
    A2bf = sbt("A2bf", [128, D], BF16)
    stats = sbt("stats", [128, 80], F32)
    st2 = sbt("st2", [128, 8], F32)
    zz = sbt("zz", [128, 8], F32)
    rmax = sbt("rmax", [128, 1], F32)
    gtiny = sbt("gtiny", [1, 16], F32)
    xs_col = sbt("xs_col", [128, 1], F32)
    sepb = sbt("sepb", [128, 32], F32)
    xinv_col = sbt("xinv_col", [128, 1], F32)

    ident_bf = sbt("identbf", [128, 128], BF16)
    ident_f = sbt("identf", [128, 128], F32)
    mask_f = sbt("maskf", [128, 128], F32)
    ipat_f = sbt("ipatf", [128, 128], F32)
    ones2 = sbt("sones2", [128, 2], F32)
    onesr_bf = sbt("sonesrbf", [1, 128], BF16)
    onesr_f = sbt("sonesrf", [1, 128], F32)
    kcol = sbt("skcol", [128, 4], F32)
    bcols = {n: sbt("s" + n, [128, CH], F32) for n in ["qb1c", "kb1c", "vbc", "gbc"]}
    brows = {n: sbt("s" + n, [1, D], BF16) for n in ["b2q", "b2k", "bp"]}
    bias_rep = sbt("biasrep", [128, D], F32)
    ws_rep = sbt("wsrep", [128, D], F32)
    wsx = bias_rep.ap()

    assert nc.sbuf_bytes_remaining >= 0, nc.sbuf_bytes_remaining

    bufA_f = bufA.ap().bitcast(F32)
    xp_t = [bufA_f[:, i * D:(i + 1) * D] for i in range(2)]
    xn2_t = [bufA_f[:, (2 + i) * D:(3 + i) * D] for i in range(2)]
    xb_t = [bufA_f[:, (4 + i) * D:(5 + i) * D] for i in range(2)]
    bufD_f = bufD.ap().bitcast(F32)
    lnx = [bufD_f[:, (4 + i) * D:(5 + i) * D] for i in range(4)]
    c_ln2 = [bufD_f[:, 2 * i * D:(2 * i + 2) * D] for i in range(2)]
    bufD_b = bufD.ap()
    c_xq = [bufD_b[:, (8 + 2 * i) * D:(10 + 2 * i) * D] for i in range(2)]
    c_xqf = [bufD_b[:, (12 + 2 * i) * D:(14 + 2 * i) * D] for i in range(2)]
    fwi8 = bufC.ap().bitcast(I8)[:, 0:CH * D]
    bufC_f = bufC.ap().bitcast(F32)
    cscr2 = bufC_f[:, 2 * D:4 * D]
    c_xb = [bufC_f[:, (4 + 2 * i) * D:(6 + 2 * i) * D] for i in range(2)]
    bufA_f2 = bufA.ap().bitcast(F32)
    c_out = [bufA_f2[:, 2 * i * D:(2 * i + 2) * D] for i in range(2)]

    pBig = pst("pBig", [128, 1024])
    pBig2 = pst("pBig2", [128, 1024])
    pP = pst("pP", [128, 512])
    pT = pst("pT", [128, 512])
    pA = pst("pA", [128, 1024])
    assert nc.psum_banks_remaining == 0

    dma_sems = ["KC", "KB0", "KB1", "KB2", "KF", "KG", "A0", "A1", "X0", "X1",
                "XL", "SC", "N0", "N1", "M0", "M1", "L0", "L1", "B0", "B1", "O0", "O1"]
    eng_sems = ["TE", "VE", "SE", "CC", "GE"]
    sems = {n: est.enter_context(nc.semaphore(n)) for n in dma_sems + eng_sems}
    cnt = {n: 0 for n in sems}
    rel = {}

    def inc(instr, s, n=1):
        instr.then_inc(sems[s], n)
        cnt[s] += n

    def wt(eng, *ss):
        for s in ss:
            eng.wait_ge(sems[s], cnt[s])

    def use(eng, *res):
        for r in res:
            for s, c in rel.get(r, ()):
                eng.wait_ge(sems[s], c)

    def release(r, s):
        rel[r] = [(s, cnt[s])]

    def release_multi(r, ss):
        rel[r] = [(s, cnt[s]) for s in ss]

    def use_l(eng, names):
        for n in names:
            use(eng, n)

    def rel_l(names, s):
        for n in names:
            release(n, s)

    blk = est.enter_context(nc.Block())

    class _RecInstr:
        __slots__ = ("incs",)
        def __init__(self):
            self.incs = []
        def then_inc(self, sem, n=1):
            self.incs.append((sem, n))
            return self

    class _RecEngine:
        __slots__ = ("_calls",)
        def __init__(self, calls):
            self._calls = calls
        def __getattr__(self, name):
            calls = self._calls
            def call(*a, **k):
                ri = _RecInstr()
                calls.append((name, a, k, ri))
                return ri
            return call

    _recs = {n: [] for n in ("sync", "gpsimd", "tensor", "vector", "scalar")}

    def on(engname):
        def deco(fn):
            fn(_RecEngine(_recs[engname]))
        return deco

    def flush_program():
        for engname, calls in _recs.items():
            if not calls:
                continue
            def body(eng, calls=calls):
                seen = {}
                for name, a, k, ri in calls:
                    if name == "wait_ge" and not ri.incs:
                        sem, val = a
                        if seen.get(id(sem), -1) >= val:
                            continue
                        seen[id(sem)] = val
                    instr = getattr(eng, name)(*a, **k)
                    for sem, n in ri.incs:
                        instr.then_inc(sem, n)
            getattr(blk, engname)(body)

    def vsep(v):
        v.tensor_copy(sepb[:], sepb[:])

    def fin():
        @on("gpsimd")
        def _(g):
            wt(g, *dma_sems)
            wt(g, "CC")
        flush_program()
        est.close()
        return nc

    def tap(name, src, *waits):
        if not DEBUG:
            return

        @on("gpsimd")
        def _(g):
            wt(g, *waits)
            inc(g.dma_start(out=taps[name], in_=src), "KG", 16)
        for en in ("vector", "scalar", "tensor"):
            @on(en)
            def _(e):
                wt(e, "KG")

    wseq = []
    for _s in range(S):
        wseq += ["wq1t", "wk1t", "wq2t", "wk2t", "wvt", "wgt", "wpt"]
    arena_next = [0]
    arena_ready = {}

    def arena_load_next():
        e = arena_next[0]
        if e >= len(wseq):
            return
        if NO_ARENA:
            arena_next[0] += 1
            arena_ready[e] = 0
            return
        arena_next[0] += 1
        slot, name = e % 2, wseq[e]

        @on("gpsimd")
        def _(g):
            use(g, f"arena{slot}")
            src = wdram[name].rearrange("(k p) n -> p k n", p=128)
            i = g.dma_start(out=arena[slot].ap().rearrange("p (k n) -> p k n", k=CH), in_=src)
            inc(i, f"A{slot}", 16)
            arena_ready[e] = cnt[f"A{slot}"]

    def arena_wait(eng, e):
        eng.wait_ge(sems[f"A{e % 2}"], arena_ready[e])

    def arena_release(e, sem):
        release(f"arena{e % 2}", sem)

    @on("sync")
    def _(sy):
        pairs = [(ident_bf, ident_bf_d), (ident_f, ident_f_d), (mask_f, mask_f_d),
                 (ipat_f, ipat_f_d), (ones2, ones2_d), (onesr_bf, onesr_bf_d),
                 (onesr_f, onesr_f_d), (bias_rep, bias_rep_d), (ws_rep, ws_rep_d),
                 (kcol, kcol_d)]
        pairs += [(bcols[n], bias_d[n]) for n in bcols]
        for dst, src in pairs:
            inc(sy.dma_start(out=dst[:], in_=src), "KC", 16)

    @on("gpsimd")
    def _(g):
        for bi, n in enumerate(brows):
            inc(g.dma_start(out=brows[n][:], in_=row_d[n]), f"KB{bi}", 16)

    @on("vector")
    def _(v):
        wt(v, "KC")
        v.memset(sepb[:], 0.0)
        inc(v.memset(rmax[:], 0.0), "VE")

    @on("tensor")
    def _(t):
        wt(t, "KC", "KB0", "KB1", "KB2")

    arena_load_next()
    arena_load_next()

    def proj_fm(eidx, src_wide, dst_wide, bias_cols, entry=None):
        if entry is None:
            entry = [("SE", cnt["SE"]), ("VE", cnt["VE"])]
        rot = [(pP[:, 0:512], "pP"), (pT[:, 0:512], "pT"),
               (pBig[:, 0:512], "pBigA"), (pBig[:, 512:1024], "pBigB"),
               (pBig2[:, 0:512], "pBig2A"), (pBig2[:, 512:1024], "pBig2B")]
        for m in range(CH):
            for sl in range(4):
                pp, pn = rot[(m * 4 + sl) % 6]

                first = (m == 0 and sl == 0)

                @on("tensor")
                def _(t, m=m, sl=sl, pp=pp, pn=pn, first=first):
                    if first:
                        arena_wait(t, eidx)
                        for sm_, c_ in entry:
                            t.wait_ge(sems[sm_], c_)
                    use(t, pn)
                    i = None
                    for k in range(CH):
                        i = t.matmul(pp,
                                     arena[eidx % 2].ap()[:, k * D + m * 128:k * D + (m + 1) * 128],
                                     src_wide[:, k * T + sl * 512:k * T + (sl + 1) * 512],
                                     start=(k == 0), stop=(k == CH - 1))
                    inc(i, "TE")

                @on("scalar")
                def _(s, m=m, sl=sl, pp=pp, pn=pn):
                    wt(s, "TE")
                    i = s.activation(dst_wide[:, m * T + sl * 512:m * T + (sl + 1) * 512],
                                     pp, AF.Gelu, bias=bias_cols[:, m:m + 1])
                    inc(i, "SE")
                    release(pn, "SE")
        arena_release(eidx, "TE")

    slot_sem = {"xt0": "X0", "xt1": "X1", "lnsA": "XL"}

    def ln1_stats_unit(sq, w, slot_ap, slot_name):
        xpv = xin.rearrange("(s u r) d -> s r u d", s=S, r=16)
        sm = slot_sem[slot_name]

        @on("sync")
        def _(sy):
            use(sy, slot_name)
            inc(sy.dma_start(out=slot_ap, in_=xpv[sq, w]), sm, 16)

        @on("scalar")
        def _(sc):
            wt(sc, sm)
            sc.activation(scrf[:], slot_ap, AF.Copy, accum_out=stats[:, w:w + 1])
            i = sc.activation(scrf[:], slot_ap, AF.Square, accum_out=stats[:, 16 + w:17 + w])
            inc(i, "SE")
            release(slot_name, "SE")

    A_sb_tile = A_sb.ap()[:, 0:D]
    for s_ in range(S):
        eb = 7 * s_
        xperm = xin.rearrange("(s u r) d -> s r u d", s=S, r=16)
        xnat = xin.rearrange("(s w p) d -> s w p d", s=S, p=128)

        if s_ == 0:
            for w in range(W):
                ln1_stats_unit(s_, w, xt[w % 2][:], f"xt{w % 2}")

        @on("vector")
        def _(v):
            wt(v, "SE")
            v.tensor_scalar(stats[:, 32:48], stats[:, 0:16], 1.0 / D, None, ALU.mult)
            v.tensor_scalar(stats[:, 48:64], stats[:, 16:32], 1.0 / D, None, ALU.mult)
            v.tensor_tensor(stats[:, 0:16], stats[:, 32:48], stats[:, 32:48], ALU.mult)
            vsep(v)
            i = v.tensor_tensor(stats[:, 16:32], stats[:, 48:64], stats[:, 0:16], ALU.subtract)
            inc(i, "VE")

        @on("scalar")
        def _(s):
            wt(s, "VE")
            s.activation(stats[:, 0:16], stats[:, 16:32], AF.Ln, bias=kcol[:, 0:1])
            i = s.activation(stats[:, 48:64], stats[:, 0:16], AF.Exp, scale=kcol[:, 1:2])
            inc(i, "SE")

        @on("vector")
        def _(v):
            wt(v, "SE")
            v.scalar_tensor_tensor(stats[:, 64:80], stats[:, 32:48], -1.0, stats[:, 48:64],
                                   ALU.mult, ALU.mult)
            vsep(v)
            i = v.tensor_copy(sepb[:], sepb[:])
            inc(i, "VE")

        if TRUNC == 1 and s_ == 0:
            tap("tap_stats", stats[:], "VE", "SE")
            return fin()
        for w in range(W):
            @on("sync")
            def _(sy, w=w):
                use(sy, f"xt{w % 2}")
                inc(sy.dma_start(out=xt[w % 2][:], in_=xperm[s_, w]), f"X{w % 2}", 16)

            @on("vector")
            def _(v, w=w):
                wt(v, f"X{w % 2}")
                use(v, f"xnt{w % 2}")
                i = v.tensor_scalar(xnt[w % 2][:], xt[w % 2][:],
                                    stats[:, 48 + w:49 + w], stats[:, 64 + w:65 + w],
                                    ALU.mult, ALU.add)
                inc(i, "VE")
                release(f"xt{w % 2}", "VE")

            for g2 in range(2):
                rot4 = [(pP[:, 0:512], "pP"), (pT[:, 0:512], "pT"),
                        (pBig[:, 0:512], "pBigA"), (pBig[:, 512:1024], "pBigB")]
                pp, pn = rot4[(w * 2 + g2) % 4]

                @on("tensor")
                def _(t, w=w, g2=g2, pp=pp, pn=pn):
                    wt(t, "VE")
                    use(t, pn)
                    i = None
                    for mm_ in range(4):
                        m = g2 * 4 + mm_
                        i = t.matmul(pp[:, mm_ * 128:(mm_ + 1) * 128],
                                     xnt[w % 2][:, m * 128:(m + 1) * 128], ident_bf[:],
                                     start=(mm_ == 0), stop=(mm_ == 3), skip_group_check=True)
                    inc(i, "TE")

                @on("vector")
                def _(v, w=w, g2=g2, pp=pp, pn=pn):
                    wt(v, "TE")
                    use(v, "bufA")
                    dst = bufA.ap().rearrange("p (c t) -> p c t", c=CH)[:, g2 * 4:(g2 + 1) * 4,
                                                                       w * 128:(w + 1) * 128]
                    i = v.tensor_copy(dst, pp.rearrange("p (c u) -> p c u", c=4))
                    inc(i, "VE")
                    release(pn, "VE")
                    if g2 == 1:
                        release(f"xnt{w % 2}", "VE")

        if s_ == 0:
            tap("tap_xn", bufA.ap()[:], "VE")
            tap("tap_stats", stats[:], "VE", "SE")
        if TRUNC == 2:
            return fin()
        proj_fm(eb + 0, bufA.ap(), bufB.ap(), bcols["qb1c"])
        arena_load_next()
        proj_fm(eb + 1, bufA.ap(), bufC.ap(), bcols["kb1c"])
        arena_load_next()
        if s_ == 0:
            tap("tap_q1", bufB.ap()[:], "SE")
            tap("tap_k1", bufC.ap()[:], "SE")
        if TRUNC == 3:
            return fin()

        sp_done = {}

        def a_mms(w):
            @on("tensor")
            def _(t, w=w):
                t.wait_ge(sems["SE"], sp_done[w])
                i = None
                for j in range(CH):
                    i = t.matmul(pA[:, j * 128:(j + 1) * 128], ktm[w % 2][:, j * 128:(j + 1) * 128],
                                 qtm[w % 2][:, j * 128:(j + 1) * 128],
                                 start=(w == 0 and j in (0, 4)), stop=(w == W - 1),
                                 skip_group_check=True)
                inc(i, "TE")
                release(f"qtm{w % 2}", "TE")
                release(f"ktm{w % 2}", "TE")

        for w in range(W):
            for qk in range(2):
                src_wide = bufB.ap() if qk == 0 else bufC.ap()
                eidx = eb + 2 + qk
                brow = brows["b2q"] if qk == 0 else brows["b2k"]
                slot = (2 * w + qk) % 3
                if slot == 0:
                    halves, pnames = [pBig[:, 0:512], pBig[:, 512:1024]], ["pBigA", "pBigB"]
                elif slot == 1:
                    halves, pnames = [pBig2[:, 0:512], pBig2[:, 512:1024]], ["pBig2A", "pBig2B"]
                else:
                    halves, pnames = [pP[:, 0:512], pT[:, 0:512]], ["pP", "pT"]
                dsttm = qtm if qk == 0 else ktm
                dname = ("qtm" if qk == 0 else "ktm") + str(w % 2)
                first = (w == 0 and qk == 0)

                @on("tensor")
                def _(t, w=w, src_wide=src_wide, eidx=eidx, brow=brow, halves=halves, pnames=pnames, first=first):
                    if w == 0:
                        arena_wait(t, eidx)
                    if first:
                        wt(t, "SE")
                    use_l(t, pnames)
                    for k in range(CH):
                        for hf in range(2):
                            t.matmul(halves[hf],
                                     src_wide[:, k * T + w * 128:k * T + (w + 1) * 128],
                                     arena[eidx % 2].ap()[:, k * D + hf * 512:k * D + (hf + 1) * 512],
                                     start=(k == 0), stop=False)
                    i = None
                    for hf in range(2):
                        i = t.matmul(halves[hf], onesr_bf[:],
                                     brow[:, hf * 512:(hf + 1) * 512], start=False, stop=True)
                    inc(i, "TE")

                @on("scalar")
                def _(s, w=w, halves=halves, pnames=pnames, dsttm=dsttm, dname=dname, qk=qk, slot=slot):
                    wt(s, "TE")
                    use(s, dname)
                    if slot == 2:
                        s.activation(scrf[:, 0:512], halves[0], AF.Exp)
                        s.activation(scrf[:, 512:1024], halves[1], AF.Exp)
                    else:
                        s.activation(scrf[:], halves[0].tensor.ap() if False else (pBig[:] if slot == 0 else pBig2[:]), AF.Exp)
                    i = s.activation(dsttm[w % 2][:], scrf[:], AF.Ln, bias=1.0)
                    inc(i, "SE")
                    for pn in pnames:
                        release(pn, "SE")
                    if qk == 1:
                        sp_done[w] = cnt["SE"]

            if w > 0:
                a_mms(w - 1)
            if s_ == 0 and w == 0:
                tap("tap_qtm", qtm[0].ap()[:], "SE")
                tap("tap_ktm", ktm[0].ap()[:], "SE")
        a_mms(W - 1)
        arena_release(eb + 2, "TE")
        arena_release(eb + 3, "TE")
        arena_load_next()
        arena_load_next()

        @on("vector")
        def _(v):
            wt(v, "TE")
            v.tensor_copy(A_sb[:], pA[:])
            i = None
            for j in range(CH):
                sl = slice(j * 128, (j + 1) * 128)
                v.tensor_tensor(scrf[:, sl], A_sb[:, sl], mask_f[:], ALU.mult)
                i = v.tensor_tensor(A2bf.ap()[:, sl], scrf[:, sl], ipat_f[:], ALU.add)
            inc(i, "VE")

        @on("tensor")
        def _(t):
            wt(t, "VE")
            use(t, "pT")
            i = None
            for j in range(CH):
                i = t.matmul(pT[:, 2 * j:2 * j + 2], A_sb[:, j * 128:(j + 1) * 128], ones2[:],
                             start=(j == 0), stop=(j == CH - 1), skip_group_check=True)
            inc(i, "TE")

        @on("vector")
        def _(v):
            wt(v, "TE")
            release("lnsA", "VE")
            zv = pT[:, 0:16].rearrange("p (j two) -> p two j", two=2)
            v.tensor_copy(zz[0:64, :], zv[0:64, 0, :])
            v.tensor_copy(zz[64:128, :], zv[64:128, 1, :])
            vsep(v)
            v.tensor_scalar(zz[:], zz[:], SCALE, float(D), ALU.mult, ALU.add)
            vsep(v)
            i = v.reciprocal(zz[:], zz[:])
            inc(i, "VE")
            release("pT", "VE")

        if s_ == 0:
            tap("tap_Asb", A_sb[:], "VE")
            tap("tap_A2", A2bf[:], "VE")
            tap("tap_zz", zz[:], "VE")
        if TRUNC == 4:
            return fin()
        proj_fm(eb + 4, bufA.ap(), bufB.ap(), bcols["vbc"])
        arena_load_next()
        proj_fm(eb + 5, bufA.ap(), bufC.ap(), bcols["gbc"])
        arena_load_next()

        rotB = [(pP[:, 0:512], "pP"), (pT[:, 0:512], "pT"),
                (pBig[:, 0:512], "pBigA"), (pBig[:, 512:1024], "pBigB")]
        for sl in range(4):
            for j in range(CH):
                pp, pn = rotB[(sl * CH + j) % 4]

                first = (j == 0 and sl == 0)

                @on("tensor")
                def _(t, j=j, sl=sl, pp=pp, pn=pn, first=first):
                    if first:
                        wt(t, "VE", "SE")
                    use(t, pn)
                    i = t.matmul(pp, A2bf.ap()[:, j * 128:(j + 1) * 128],
                                 bufB.ap()[:, j * T + sl * 512:j * T + (sl + 1) * 512],
                                 start=True, stop=True)
                    inc(i, "TE")

                @on("vector")
                def _(v, j=j, sl=sl, pp=pp, pn=pn):
                    wt(v, "TE")
                    i = v.scalar_tensor_tensor(
                        bufA.ap()[:, j * T + sl * 512:j * T + (sl + 1) * 512],
                        pp, zz[:, j:j + 1],
                        bufC.ap()[:, j * T + sl * 512:j * T + (sl + 1) * 512],
                        ALU.mult, ALU.mult)
                    inc(i, "VE")
                    release(pn, "VE")

            @on("sync")
            def _(sy, sl=sl):
                wt(sy, "VE")
                if sl == 0:
                    use(sy, "bufD")
                for jj in (2 * sl, 2 * sl + 1):
                    for h2 in range(2):
                        for r2 in range(2):
                            src = bufA.ap()[h2 * 64:(h2 + 1) * 64, :].rearrange(
                                "p (hp c) -> p hp c", hp=CH)[:, :, (2 * jj + r2) * 128:(2 * jj + r2 + 1) * 128]
                            dst = bufD.ap()[r2 * 64:(r2 + 1) * 64, jj * T:(jj + 1) * T].rearrange(
                                "p (hp h2x u) -> p hp h2x u", hp=CH, h2x=2)[:, :, h2, :]
                            inc(sy.dma_start(out=dst, in_=src), "SC", 16)

        if s_ == 0:
            tap("tap_v", bufB.ap()[:], "SE")
            tap("tap_g", bufC.ap()[:], "SE")
            tap("tap_E", bufA.ap()[:], "VE")

        if s_ == 0:
            tap("tap_Es", bufD.ap()[:], "SC")
        if TRUNC == 5:
            return fin()
        for w in range(W):
            @on("sync")
            def _(sy, w=w):
                use(sy, f"xt{w % 2}")
                inc(sy.dma_start(out=xt[w % 2][:], in_=xnat[s_, w]), f"X{w % 2}", 16)

            ppm, ppmn = (pBig, ["pBigA", "pBigB"]) if w % 2 == 0 else (pBig2, ["pBig2A", "pBig2B"])

            @on("tensor")
            def _(t, w=w, ppm=ppm, ppmn=ppmn):
                if w == 0:
                    arena_wait(t, eb + 6)
                    wt(t, "SC")
                wt(t, f"X{w % 2}")
                use_l(t, ppmn)
                for k in range(CH):
                    for hf in range(2):
                        t.matmul(ppm[:, hf * 512:(hf + 1) * 512],
                                 bufD.ap()[:, k * T + w * 128:k * T + (w + 1) * 128],
                                 arena[(eb + 6) % 2].ap()[:, k * D + hf * 512:k * D + (hf + 1) * 512],
                                 start=(k == 0), stop=False)
                i = None
                for hf in range(2):
                    i = t.matmul(ppm[:, hf * 512:(hf + 1) * 512], onesr_bf[:],
                                 brows["bp"][:, hf * 512:(hf + 1) * 512], start=False, stop=True)
                inc(i, "TE")

            @on("vector")
            def _(v, w=w, ppm=ppm, ppmn=ppmn):
                wt(v, "TE")
                use(v, f"xp{w % 2}")
                i = v.tensor_tensor(xp_t[w % 2], ppm[:], xt[w % 2][:], ALU.add)
                inc(i, "VE")
                rel_l(ppmn, "VE")
                release(f"xt{w % 2}", "VE")

            @on("scalar")
            def _(s, w=w):
                wt(s, "VE")
                s.activation(scrf[:], xp_t[w % 2], AF.Copy, accum_out=st2[:, 0:1])
                i = s.activation(scrf[:], xp_t[w % 2], AF.Square, accum_out=st2[:, 1:2])
                inc(i, "SE")

            @on("vector")
            def _(v, w=w):
                wt(v, "SE")
                v.tensor_scalar(st2[:, 2:3], st2[:, 0:1], 1.0 / D, None, ALU.mult)
                v.tensor_scalar(st2[:, 3:4], st2[:, 1:2], 1.0 / D, None, ALU.mult)
                v.tensor_tensor(st2[:, 4:5], st2[:, 2:3], st2[:, 2:3], ALU.mult)
                vsep(v)
                i = v.tensor_tensor(st2[:, 4:5], st2[:, 3:4], st2[:, 4:5], ALU.subtract)
                inc(i, "VE")

            @on("scalar")
            def _(s, w=w):
                wt(s, "VE")
                s.activation(st2[:, 5:6], st2[:, 4:5], AF.Ln, bias=kcol[:, 0:1])
                i = s.activation(st2[:, 5:6], st2[:, 5:6], AF.Exp, scale=kcol[:, 1:2])
                inc(i, "SE")

            @on("vector")
            def _(v, w=w):
                wt(v, "SE")
                v.scalar_tensor_tensor(st2[:, 6:7], st2[:, 2:3], -1.0, st2[:, 5:6],
                                       ALU.mult, ALU.mult)
                vsep(v)
                i = v.tensor_copy(sepb[:], sepb[:])
                inc(i, "VE")

            @on("vector")
            def _(v, w=w):
                use(v, f"xn2{w % 2}", f"xb{w % 2}")
                v.tensor_scalar(xn2_t[w % 2], xp_t[w % 2], st2[:, 5:6], st2[:, 6:7],
                                ALU.mult, ALU.add)
                v.reduce_max(st2[:, 7:8], xn2_t[w % 2], axis=AX.X, apply_absolute_value=True)
                vsep(v)
                i = v.tensor_tensor(rmax[:], rmax[:], st2[:, 7:8], ALU.max)
                inc(i, "VE")

            if s_ == 0 and w == 0:
                tap("tap_xp", xp_t[0], "VE")

            @on("gpsimd")
            def _(g, w=w):
                wt(g, "VE")
                use(g, f"xb{w % 2}")
                i = g.tensor_tensor(xb_t[w % 2], xp_t[w % 2], bias_rep[:], ALU.add)
                inc(i, "GE", 1)
                release(f"xp{w % 2}", "GE")
                row = s_ * T + w * 128
                inc(g.dma_start(out=scr_xn2[row:row + 128, :], in_=xn2_t[w % 2]), f"N{w % 2}", 16)
                release(f"xn2{w % 2}", f"N{w % 2}")
                inc(g.dma_start(out=scr_xb[row:row + 128, :], in_=xb_t[w % 2]), f"M{w % 2}", 16)
                release(f"xb{w % 2}", f"M{w % 2}")
                if w == W - 1:
                    release_multi("bufA", ["N0", "N1", "M0", "M1"])

            if s_ + 1 < S:
                ln1_stats_unit(s_ + 1, w, A_sb_tile, "lnsA")

        arena_release(eb + 6, "TE")
        arena_load_next()
        release("bufD", "TE")

    pre_cc = {n: cnt[n] for n in ["N0", "N1", "M0", "M1", "TE", "VE", "SE"]}

    @on("gpsimd")
    def _(g):
        wt(g, "VE", "SE", "TE")
        src = fwt.rearrange("(k p) n -> p k n", p=128)
        inc(g.dma_start(out=fwi8.rearrange("p (k n) -> p k n", k=CH), in_=src), "KF", 16)

    @on("vector")
    def _(v):
        wt(v, "KF")
        inc(v.tensor_copy(arena[0].ap(), fwi8), "VE")

    @on("tensor")
    def _(t):
        wt(t, "VE")
        use(t, "pT")
        i = t.matmul(pT[0:1, 0:128], rmax[:], ident_f[:], start=True, stop=True)
        inc(i, "TE")

    @on("vector")
    def _(v):
        wt(v, "TE")
        inc(v.reduce_max(gtiny[0:1, 0:1], pT[0:1, 0:128], axis=AX.X), "VE")

    @on("gpsimd")
    def _(g):
        wt(g, "VE")
        inc(g.dma_start(out=cc_in.ap(), in_=gtiny[0:1, 0:1]), "KG", 16)
        wt(g, "KG")
        inc(g.collective_compute("AllGather", ALU.bypass, replica_groups=[list(range(8))],
                                 ins=[cc_in.ap().opt()], outs=[cc_out.ap().opt()]), "CC", 1)
        wt(g, "CC")
        inc(g.dma_start(out=gtiny[0:1, 1:9], in_=cc_out.ap()), "KG", 16)

    @on("vector")
    def _(v):
        wt(v, "KG")
        v.reduce_max(gtiny[0:1, 9:10], gtiny[0:1, 1:9], axis=AX.X)
        vsep(v)
        v.tensor_scalar(gtiny[0:1, 10:11], gtiny[0:1, 9:10], 1.0 / 127.0, EPS, ALU.mult, ALU.max)
        vsep(v)
        inc(v.reciprocal(gtiny[0:1, 11:12], gtiny[0:1, 10:11]), "VE")

    @on("tensor")
    def _(t):
        wt(t, "VE")
        i = t.matmul(pT[:, 0:2], onesr_f[:], gtiny[0:1, 10:12], start=True, stop=True)
        inc(i, "TE")

    @on("vector")
    def _(v):
        wt(v, "TE")
        v.tensor_copy(xs_col[:], pT[:, 0:1])
        v.tensor_copy(xinv_col[:], pT[:, 1:2])
        vsep(v)
        i = v.tensor_scalar(wsx, ws_rep[:], xs_col[:], None, ALU.mult)
        inc(i, "VE")
        release("pT", "VE")

    fw_bf = arena[0].ap()

    xs_ready = ("VE", cnt["VE"])

    @on("sync")
    def _(sy):
        for n in ("N0", "N1", "M0", "M1"):
            sy.wait_ge(sems[n], pre_cc[n])
        sy.wait_ge(sems["TE"], pre_cc["TE"])
        use(sy, "bufD")

    @on("scalar")
    def _(sc):
        sc.wait_ge(sems["VE"], pre_cc["VE"])
        sc.wait_ge(sems["TE"], pre_cc["TE"])
        for n in ("M0", "M1"):
            sc.wait_ge(sems[n], pre_cc[n])

    for t_ in range(S * W // 2):
        row = t_ * 256
        pg = t_ % 2

        @on("sync")
        def _(sy, row=row, pg=pg):
            use(sy, f"cln2{pg}")
            src = scr_xn2.rearrange("(n two p) d -> n p two d", two=2, p=128)[row // 256]
            inc(sy.dma_start(out=c_ln2[pg].rearrange("p (two d) -> p two d", two=2), in_=src), f"L{pg}", 16)

        @on("scalar")
        def _(sc, row=row, pg=pg):
            use(sc, f"cxb{pg}")
            src = scr_xb.rearrange("(n two p) d -> n p two d", two=2, p=128)[row // 256]
            inc(sc.dma_start(out=c_xb[pg].rearrange("p (two d) -> p two d", two=2), in_=src), f"B{pg}", 16)

        @on("scalar")
        def _(sc, pg=pg):
            wt(sc, f"L{pg}", f"B{pg}")
            sc.wait_ge(sems[xs_ready[0]], xs_ready[1])
            i = sc.activation(c_ln2[pg], c_ln2[pg], AF.Identity, scale=xinv_col[:], bias=kcol[:, 2:3])
            inc(i, "SE")

        @on("vector")
        def _(v, pg=pg):
            wt(v, "SE")
            use(v, f"cxq{pg}")
            i = v.tensor_scalar(c_xq[pg], c_ln2[pg], MAGIC, None, ALU.subtract)
            inc(i, "VE")
            release(f"cln2{pg}", "VE")

        rotC = [(pP, "pP"), (pT, "pT"), (pA[:, 0:512], "pA0"), (pA[:, 512:1024], "pA1")]
        for g2 in range(4):
            pp, pn = rotC[g2]

            @on("tensor")
            def _(t, pg=pg, g2=g2, pp=pp, pn=pn):
                if g2 == 0:
                    wt(t, "VE")
                use(t, pn)
                i = None
                for mm_ in range(4):
                    m = g2 * 4 + mm_
                    i = t.matmul(pp[:, mm_ * 128:(mm_ + 1) * 128],
                                 c_xq[pg][:, m * 128:(m + 1) * 128], ident_bf[:],
                                 start=(mm_ == 0), stop=(mm_ == 3), skip_group_check=True)
                inc(i, "TE")

            @on("scalar")
            def _(sc, pg=pg, g2=g2, pp=pp, pn=pn):
                wt(sc, "TE")
                if g2 == 0:
                    use(sc, f"cxqf{pg}")
                i = sc.activation(c_xqf[pg][:, g2 * 512:(g2 + 1) * 512], pp[:, 0:512], AF.Copy)
                inc(i, "SE")
                release(pn, "SE")
                if g2 == 3:
                    release(f"cxq{pg}", "SE")

        for tau in range(2):
            pfm, pfn = (pBig, ["pBigA", "pBigB"]) if tau == 0 else (pBig2, ["pBig2A", "pBig2B"])

            @on("tensor")
            def _(t, pg=pg, tau=tau, pfm=pfm, pfn=pfn):
                if tau == 0:
                    wt(t, "SE")
                use_l(t, pfn)
                i = None
                for k in range(CH):
                    for hf in range(2):
                        i = t.matmul(pfm[:, hf * 512:(hf + 1) * 512],
                                     c_xqf[pg][:, tau * 1024 + k * 128:tau * 1024 + (k + 1) * 128],
                                     fw_bf[:, k * D + hf * 512:k * D + (hf + 1) * 512],
                                     start=(k == 0), stop=(k == CH - 1))
                inc(i, "TE")
                if tau == 1:
                    release(f"cxqf{pg}", "TE")

            @on("vector")
            def _(v, pg=pg, tau=tau, pfm=pfm, pfn=pfn):
                wt(v, "TE")
                use(v, f"cscr{tau}")
                i = v.tensor_tensor(cscr2[:, tau * D:(tau + 1) * D], pfm[:], wsx, ALU.mult)
                inc(i, "VE")
                rel_l(pfn, "VE")

            if tau == 0:
                @on("vector")
                def _(v, pg=pg, tau=tau):
                    use(v, f"cout{pg}")
                    i = v.tensor_tensor(c_out[pg][:, tau * D:(tau + 1) * D],
                                        cscr2[:, tau * D:(tau + 1) * D], c_xb[pg][:, tau * D:(tau + 1) * D],
                                        ALU.add)
                    inc(i, "VE")
                    release(f"cscr{tau}", "VE")
            else:
                @on("gpsimd")
                def _(g, pg=pg, tau=tau):
                    wt(g, "VE")
                    i = g.tensor_tensor(c_out[pg][:, tau * D:(tau + 1) * D],
                                        cscr2[:, tau * D:(tau + 1) * D], c_xb[pg][:, tau * D:(tau + 1) * D],
                                        ALU.add)
                    inc(i, "GE", 1)
                    release(f"cscr{tau}", "GE")
                    release(f"cxb{pg}", "GE")

        @on("gpsimd")
        def _(g, row=row, pg=pg):
            wt(g, "VE")
            dst = out_d.rearrange("(n two p) d -> n p two d", two=2, p=128)[row // 256]
            inc(g.dma_start(out=dst, in_=c_out[pg].rearrange("p (two d) -> p two d", two=2)), f"O{pg}", 16)
            release(f"cout{pg}", f"O{pg}")

    @on("gpsimd")
    def _(g):
        wt(g, "O0", "O1")

    flush_program()
    est.close()
    return nc


def kernel(**inputs):
    x = np.ascontiguousarray(np.asarray(inputs["L0_attn_x"]), dtype=np.float32)
    g = lambda n: np.asarray(inputs[n])
    wT = lambda n: np.ascontiguousarray(g(n).astype(np.float32).T)
    bc = lambda n: np.ascontiguousarray(g(n).astype(np.float32).reshape(CH, 128).T)

    m128 = np.kron(np.eye(2, dtype=np.float32), np.ones((64, 64), np.float32)) * np.float32(SCALE)
    i128 = np.eye(128, dtype=np.float32)
    common = dict(
        wq1t=wT("L0_attn_q_w1"), wk1t=wT("L0_attn_k_w1"),
        wq2t=wT("L0_attn_q_w2"), wk2t=wT("L0_attn_k_w2"),
        wvt=wT("L0_attn_v_w"), wgt=wT("L0_attn_g_w"), wpt=wT("L0_attn_p_w"),
        fwt=np.ascontiguousarray(g("L0_ffn_weight_int8").T),
        qb1c=bc("L0_attn_q_b1"), kb1c=bc("L0_attn_k_b1"),
        vbc=bc("L0_attn_v_b"), gbc=bc("L0_attn_g_b"),
        b2q=g("L0_attn_q_b2").reshape(1, D).astype(np.float32),
        b2k=g("L0_attn_k_b2").reshape(1, D).astype(np.float32),
        bp=g("L0_attn_p_b").reshape(1, D).astype(np.float32),
        bias_rep=np.ascontiguousarray(np.broadcast_to(g("L0_ffn_bias").reshape(1, D).astype(np.float32), (128, D))),
        ws_rep=np.ascontiguousarray(np.broadcast_to(g("L0_ffn_weight_scale").reshape(1, D).astype(np.float32), (128, D))),
        ident_bf=np.eye(128, dtype=BF),
        ident_f=i128,
        mask_f=np.ascontiguousarray(m128),
        ipat_f=i128,
        ones2=np.concatenate([np.repeat([[1.0, 0.0]], 64, 0), np.repeat([[0.0, 1.0]], 64, 0)]).astype(np.float32),
        onesr_bf=np.ones((1, 128), dtype=BF),
        onesr_f=np.ones((1, 128), dtype=np.float32),
        kcol=np.ascontiguousarray(np.broadcast_to(np.array([[EPS, -0.5, MAGIC, 0.0]], np.float32), (128, 4))),
    )

    nc = build_nc()
    in_maps = []
    for c in range(8):
        mm = dict(common)
        mm["xin"] = np.ascontiguousarray(x[4 * c:4 * c + 4].reshape(S * T, D))
        in_maps.append(mm)
    res = run_bass_kernel_spmd(nc, in_maps, core_ids=list(range(8)))
    return np.concatenate([res.results[c]["out"].reshape(4, T, D) for c in range(8)], axis=0)


if __name__ == "__main__":
    nc = build_nc()
    print("built ok; instructions:", len(nc.inst_names()) if hasattr(nc, "inst_names") else "?")
